# version 03
# baseline (speedup 1.0000x reference)
import numpy as np
import ml_dtypes
from contextlib import ExitStack
from collections import defaultdict

import concourse.bass as bass
import concourse.mybir as mybir
from concourse.bass_utils import run_bass_kernel_spmd

F32 = mybir.dt.float32
BF16 = mybir.dt.bfloat16
I8 = mybir.dt.int8
I32 = mybir.dt.int32
AF = mybir.ActivationFunctionType
ALU = mybir.AluOpType
AX = mybir.AxisListType

S = 2048
D = 4096
NH = 32
NKV = 8
HD = 128
IT = 11008
ICN = IT // 128
DCN = D // 128
TGN = 4
TW = 512
INV_SQRT_HD = float(1.0 / np.sqrt(HD))
EXP_SHIFT = -10.0
INV2PI = float(1.0 / (2.0 * np.pi))
TWOPI = float(2.0 * np.pi)

ENGINES = ("sync", "scalar", "vector", "tensor", "gpsimd")


class Prog:

    def __init__(self, nc):
        self.nc = nc
        self.ops = {e: [] for e in ENGINES}
        self.cnt = defaultdict(int)
        self.sem_handles = {}
        self.last = {e: defaultdict(int) for e in ENGINES}

    def make_sems(self, stack, *names):
        for n in names:
            self.sem_handles[n] = stack.enter_context(self.nc.semaphore(n))

    def emit(self, eng, fn, waits=None, inc=None, inc_n=1):
        incs = []
        if inc is not None:
            self.cnt[inc] += inc_n
            incs.append((inc, inc_n))
        self.ops[eng].append((dict(waits or {}), fn, incs))
        return self.cnt[inc] if inc is not None else None

    def barrier(self):
        v_sy, v_gw = self.cnt["sy"], self.cnt["gw"]
        for eng in ENGINES:
            w = {}
            if eng == "sync":
                w["sy"] = v_sy
            if eng == "gpsimd":
                w["gw"] = v_gw
            self.emit(eng, lambda e: e.drain(), waits=w, inc="ph")
        v = self.cnt["ph"]
        for eng in ENGINES:
            self.emit(eng, None, waits={"ph": v})
        return v

    def flush(self):
        nc, sems = self.nc, self.sem_handles
        with nc.Block() as block:
            for eng_name in ENGINES:
                oplist = self.ops[eng_name]
                if not oplist:
                    continue
                lastw = self.last[eng_name]

                def run(e, oplist=oplist, lastw=lastw):
                    for waits, fn, incs in oplist:
                        for s_name, v in waits.items():
                            if v <= lastw[s_name]:
                                continue
                            e.wait_ge(sems[s_name], v)
                            lastw[s_name] = v
                        if fn is None:
                            continue
                        ins = fn(e)
                        for s_name, m in incs:
                            ins.then_inc(sems[s_name], m)

                getattr(block, eng_name)(run)
        self.ops = {e: [] for e in ENGINES}


def _rot_matrix():
    R = np.zeros((128, 128), np.float32)
    for i in range(64):
        R[i, i + 64] = -1.0
        R[i + 64, i] = 1.0
    return np.ascontiguousarray(R.T).astype(ml_dtypes.bfloat16)


def host_prep(L0_attn_hidden_states, L0_attn_Wq, L0_attn_Wk, L0_attn_Wv,
              L0_attn_Wo, L0_ffn_w_gate, L0_ffn_s_gate, L0_ffn_w_up,
              L0_ffn_s_up, L0_ffn_w_down, L0_ffn_s_down):
    C = np.ascontiguousarray

    def wtile(W, ocn):
        n_in = W.shape[1]
        cn = n_in // 128
        return C(W.reshape(ocn, 128, cn, 128).transpose(0, 3, 2, 1)
                 .reshape(ocn, 128, n_in))

    shared = {
        "WqT": wtile(np.asarray(L0_attn_Wq), NH),
        "WkT": wtile(np.asarray(L0_attn_Wk), NKV),
        "WoT": wtile(np.asarray(L0_attn_Wo), DCN),
        "WvR": C(np.asarray(L0_attn_Wv).T.reshape(DCN, 128, NKV * HD)),
        "WgT": wtile(np.asarray(L0_ffn_w_gate), ICN),
        "WuT": wtile(np.asarray(L0_ffn_w_up), ICN),
        "WdT": C(np.asarray(L0_ffn_w_down).reshape(DCN, 128, ICN, 128)
                 .transpose(0, 3, 2, 1).reshape(DCN, 128, IT)),
        "sg": C(np.asarray(L0_ffn_s_gate).reshape(ICN, 128).T.astype(np.float32)),
        "su": C(np.asarray(L0_ffn_s_up).reshape(ICN, 128).T.astype(np.float32)),
        "sd": C(np.asarray(L0_ffn_s_down).reshape(DCN, 128).T.astype(np.float32)),
        "iota": np.arange(S, dtype=np.float32).reshape(1, S),
        "invf": np.concatenate([10000.0 ** (-np.arange(64, dtype=np.float64) / 64.0)] * 2
                               ).reshape(1, 128).astype(np.float32),
        "onesf": np.ones((1, 128), np.float32),
        "onescol": np.ones((128, 1), ml_dtypes.bfloat16),
        "ones128": np.ones((128, 128), ml_dtypes.bfloat16),
        "umask": np.triu(np.ones((128, 128), np.float32)).astype(ml_dtypes.bfloat16),
        "rotT": _rot_matrix(),
        "consts": np.stack([np.full(128, EXP_SHIFT, np.float32),
                            np.full(128, 1e-5, np.float32)], axis=1),
    }
    x = np.asarray(L0_attn_hidden_states)
    xTs = [C(x[b].T) for b in range(8)]
    return shared, xTs


def build_nc(upto=4, debug=False):
    nc = bass.Bass("TRN2", target_bir_lowering=False)
    scratch_kind = "ExternalOutput" if debug else "Internal"

    d = {}
    d["xT"] = nc.dram_tensor("xT", [D, S], F32, kind="ExternalInput").ap()
    d["WqT"] = nc.dram_tensor("WqT", [NH, 128, D], F32, kind="ExternalInput").ap()
    d["WkT"] = nc.dram_tensor("WkT", [NKV, 128, D], F32, kind="ExternalInput").ap()
    d["WoT"] = nc.dram_tensor("WoT", [DCN, 128, D], F32, kind="ExternalInput").ap()
    d["WvR"] = nc.dram_tensor("WvR", [DCN, 128, NKV * HD], F32, kind="ExternalInput").ap()
    d["WgT"] = nc.dram_tensor("WgT", [ICN, 128, D], I8, kind="ExternalInput").ap()
    d["WuT"] = nc.dram_tensor("WuT", [ICN, 128, D], I8, kind="ExternalInput").ap()
    d["WdT"] = nc.dram_tensor("WdT", [DCN, 128, IT], I8, kind="ExternalInput").ap()
    d["sg"] = nc.dram_tensor("sg", [128, ICN], F32, kind="ExternalInput").ap()
    d["su"] = nc.dram_tensor("su", [128, ICN], F32, kind="ExternalInput").ap()
    d["sd"] = nc.dram_tensor("sd", [128, DCN], F32, kind="ExternalInput").ap()
    d["iota"] = nc.dram_tensor("iota", [1, S], F32, kind="ExternalInput").ap()
    d["invf"] = nc.dram_tensor("invf", [1, 128], F32, kind="ExternalInput").ap()
    d["onesf"] = nc.dram_tensor("onesf", [1, 128], F32, kind="ExternalInput").ap()
    d["onescol"] = nc.dram_tensor("onescol", [128, 1], BF16, kind="ExternalInput").ap()
    d["ones128"] = nc.dram_tensor("ones128", [128, 128], BF16, kind="ExternalInput").ap()
    d["umask"] = nc.dram_tensor("umask", [128, 128], BF16, kind="ExternalInput").ap()
    d["rotT"] = nc.dram_tensor("rotT", [128, 128], BF16, kind="ExternalInput").ap()
    d["consts"] = nc.dram_tensor("consts", [128, 2], F32, kind="ExternalInput").ap()

    outT = nc.dram_tensor("outT", [D, S], F32, kind="ExternalOutput").ap()
    QS = nc.dram_tensor("QS", [TGN, NH, 128, TW], BF16, kind=scratch_kind).ap()
    KS = nc.dram_tensor("KS", [NKV, 128, S], BF16, kind=scratch_kind).ap()
    VS = nc.dram_tensor("VS", [S // 128, 128, NKV * HD], BF16, kind=scratch_kind).ap()
    OS = nc.dram_tensor("OS", [TGN, NH, 128, TW], BF16, kind=scratch_kind).ap()
    X1S = nc.dram_tensor("X1S", [TGN, DCN, 128, TW], BF16, kind=scratch_kind).ap()

    P = Prog(nc)

    with ExitStack() as top:
        P.make_sems(top, "gw", "sy", "pe", "ac", "dv", "ph")
        sbuf = lambda stk, n, sh, dt: stk.enter_context(nc.sbuf_tensor(n, sh, dt))
        PS = [top.enter_context(nc.psum_tensor(f"PS{i}", [128, 512], F32))
              for i in range(8)]

        ones_b = sbuf(top, "ones_b", [128, 1], BF16)
        rotT = sbuf(top, "rotT_s", [128, 128], BF16)
        ones128 = sbuf(top, "ones128_s", [128, 128], BF16)
        onesf = sbuf(top, "onesf_s", [1, 128], F32)
        umask = sbuf(top, "umask_s", [128, 128], BF16)
        consts = sbuf(top, "consts_s", [128, 2], F32)
        iota = sbuf(top, "iota_s", [1, S], F32)
        invf = sbuf(top, "invf_s", [1, 128], F32)
        sg = sbuf(top, "sg_s", [128, ICN], F32)
        su = sbuf(top, "su_s", [128, ICN], F32)
        sd = sbuf(top, "sd_s", [128, DCN], F32)
        c_shift = consts[:, 0:1]
        c_eps = consts[0:1, 1:2]

        for name, dst in [("onescol", ones_b), ("rotT", rotT), ("ones128", ones128),
                          ("onesf", onesf),
                          ("umask", umask),
                          ("consts", consts), ("iota", iota), ("invf", invf),
                          ("sg", sg), ("su", su), ("sd", sd)]:
            P.emit("sync", lambda e, a=d[name], b=dst: e.dma_start(out=b[:, :], in_=a),
                   inc="sy", inc_n=16)
        glob_sy = P.cnt["sy"]

        with ExitStack() as s1:
            Xfm = sbuf(s1, "Xfm", [128, DCN * S], BF16)
            for half in range(2):
                cs = slice(half * 16, (half + 1) * 16)
                P.emit("gpsimd", lambda e, cs=cs: e.dma_start(
                    out=Xfm[:, :].rearrange("p (c t) -> p c t", c=DCN)[:, cs, :],
                    in_=d["xT"].rearrange("(c p) t -> p c t", p=128)[:, cs, :]),
                    inc="gw", inc_n=16)
            x_gw = P.cnt["gw"]

            with ExitStack() as s1a:
                cosb = sbuf(s1a, "cosb", [128, S], F32)
                sinb = sbuf(s1a, "sinb", [128, S], F32)
                wqb = [sbuf(s1a, f"wqb{i}", [128, D], BF16) for i in range(2)]
                qb = [sbuf(s1a, f"qb{i}", [128, TW], BF16) for i in range(4)]
                qt = [sbuf(s1a, f"qt{i}", [128, TW], F32) for i in range(2)]
                r12 = [sbuf(s1a, f"r12_{i}", [128, TW], F32) for i in range(2)]
                qro = [sbuf(s1a, f"qro{i}", [128, TW], BF16) for i in range(2)]
                i32t = sbuf(s1a, "i32t", [128, TW], I32)

                rope_dv = 0
                for j in range(TGN):
                    sl = slice(j * TW, (j + 1) * TW)
                    mmw = {"sy": glob_sy}
                    if rope_dv:
                        mmw["dv"] = rope_dv
                    pe = P.emit("tensor", lambda e, sl=sl: e.matmul(
                        PS[0][:, :], invf[:, :], iota[:, sl], start=True, stop=True),
                        waits=mmw, inc="pe")
                    acw = 0
                    for shift, buf in [(0.25, cosb), (0.0, sinb)]:
                        w1 = {"pe": pe}
                        if acw:
                            w1["ac"] = acw
                        P.emit("vector", lambda e, shift=shift: e.tensor_scalar(
                            qt[0][:, :], PS[0][:, :], INV2PI, shift, ALU.mult, ALU.add),
                            waits=w1, inc="dv")
                        P.emit("vector", lambda e: e.tensor_copy(i32t[:, :], qt[0][:, :]),
                               inc="dv")
                        P.emit("vector", lambda e: e.tensor_copy(r12[0][:, :], i32t[:, :]),
                               inc="dv")
                        dvv = P.emit("vector", lambda e: e.tensor_sub(
                            qt[0][:, :], qt[0][:, :], r12[0][:, :]), inc="dv")
                        acw = P.emit("scalar", lambda e, sl=sl, buf=buf: e.activation(
                            buf[:, sl], qt[0][:, :], AF.Sin, bias=0.0, scale=TWOPI),
                            waits={"dv": dvv}, inc="ac")
                    rope_dv = P.cnt["dv"]
                rope_ac = P.cnt["ac"]

                pe_mark = {}
                ac_evac = {}
                rot_mark = {}
                dv_mark = {}
                st_mark = {}
                w_mark = {}
                bank = lambda tg, oc: PS[(oc % 2) * 4 + tg]
                for oc in range(NH + NKV):
                    wsrc = d["WqT"][oc] if oc < NH else d["WkT"][oc - NH]
                    gww = {}
                    if oc >= 2:
                        gww["pe"] = pe_mark[(oc - 2, 3)]
                    w_mark[oc] = P.emit("gpsimd", lambda e, wsrc=wsrc, b=oc % 2:
                                        e.dma_start(out=wqb[b][:, :], in_=wsrc),
                                        waits=gww, inc="gw", inc_n=16)
                    if oc >= 1:
                        for tg in range(TGN):
                            idx = (oc - 1) * TGN + tg
                            rot_mark[idx] = P.emit("tensor", lambda e, tg=tg, oc=oc - 1:
                                                   e.matmul(
                                bank(tg, oc)[:, :], rotT[:, :], qb[tg][:, :],
                                start=True, stop=True),
                                waits={"ac": ac_evac[(oc - 1, tg)]}, inc="pe")
                    for c in range(DCN):
                        for tg in range(TGN):
                            w = {}
                            if c == 0 and tg == 0:
                                w = {"gw": max(w_mark[oc], x_gw)}
                                if oc >= 2:
                                    w["dv"] = dv_mark[(oc - 2) * TGN + 3]
                                else:
                                    w["dv"] = rope_dv
                            last = (c == DCN - 1)
                            m = P.emit("tensor", lambda e, c=c, tg=tg, oc=oc, last=last: e.matmul(
                                bank(tg, oc)[:, :],
                                wqb[oc % 2][:, c * 128:(c + 1) * 128],
                                Xfm[:, c * S + tg * TW: c * S + tg * TW + TW],
                                start=(c == 0), stop=last),
                                waits=w, inc=("pe" if last else None))
                            if last:
                                pe_mark[(oc, tg)] = m
                    for tg in range(TGN):
                        idx = oc * TGN + tg
                        b = idx % 2
                        w = {"pe": pe_mark[(oc, tg)], "ac": rope_ac}
                        if oc >= 1:
                            w["pe"] = max(w["pe"], rot_mark[(oc - 1) * TGN + tg])
                            w["dv"] = dv_mark[(oc - 1) * TGN + tg]
                        ac_evac[(oc, tg)] = P.emit("scalar", lambda e, tg=tg, oc=oc:
                                                   e.copy(qb[tg][:, :],
                                                          bank(tg, oc)[:, :]),
                                                   waits=w, inc="ac")
                        cs = slice(tg * TW, (tg + 1) * TW)
                        w1 = {"ac": ac_evac[(oc, tg)]}
                        P.emit("vector", lambda e, b=b, tg=tg, cs=cs: e.tensor_mul(
                            qt[b][:, :], qb[tg][:, :], cosb[:, cs]), waits=w1, inc="dv")
                        def _mk_rsin(bank_ap, b, cs):
                            return lambda e: e.scalar_tensor_tensor(
                                r12[b][:, :], bank_ap, 1.0, sinb[:, cs],
                                ALU.mult, ALU.mult)
                        P.emit("vector", _mk_rsin(bank(tg, oc)[:, :], b, cs),
                               waits={"pe": ("ROTMARK", idx)}, inc="dv")
                        w3 = {}
                        if idx >= 2:
                            w3["sy"] = st_mark[idx - 2]
                        dv_mark[idx] = P.emit("vector", lambda e, b=b: e.tensor_add(
                            qro[b][:, :], qt[b][:, :], r12[b][:, :]),
                            waits=w3, inc="dv")
                        dst = (QS[tg, oc] if oc < NH
                               else KS[oc - NH][:, tg * TW:(tg + 1) * TW])
                        st_mark[idx] = P.emit("sync", lambda e, b=b, dst=dst:
                                              e.dma_start(out=dst, in_=qro[b][:, :]),
                                              waits={"dv": dv_mark[idx]},
                                              inc="sy", inc_n=16)
                oc = NH + NKV - 1
                for tg in range(TGN):
                    idx = oc * TGN + tg
                    rot_mark[idx] = P.emit("tensor", lambda e, tg=tg, oc=oc: e.matmul(
                        bank(tg, oc)[:, :], rotT[:, :], qb[tg][:, :],
                        start=True, stop=True),
                        waits={"ac": ac_evac[(oc, tg)]}, inc="pe")
                for eng in ENGINES:
                    for waits, fn, incs in P.ops[eng]:
                        for s_name, v in list(waits.items()):
                            if isinstance(v, tuple) and v[0] == "ROTMARK":
                                waits[s_name] = rot_mark[v[1]]
                P.barrier()
                P.flush()

            with ExitStack() as s1b:
                vw = sbuf(s1b, "vw", [128, DCN * NKV * HD], BF16)
                vsb = [sbuf(s1b, f"vsb{i}", [128, NKV * HD], BF16) for i in range(2)]
                vw_mark = P.emit("gpsimd", lambda e: e.dma_start(
                    out=vw[:, :].rearrange("p (c v) -> p c v", c=DCN),
                    in_=d["WvR"].rearrange("c p v -> p c v")), inc="gw", inc_n=16)
                vpe = {}
                vac = {}
                vst = {}
                for tb in range(S // 128):
                    for dvh in range(2):
                        idx = tb * 2 + dvh
                        w = {}
                        if idx == 0:
                            w["gw"] = vw_mark
                        if idx >= 2:
                            w["ac"] = vac[idx - 2]
                        for c in range(DCN):
                            last = (c == DCN - 1)
                            m = P.emit("tensor", lambda e, c=c, tb=tb, dvh=dvh, idx=idx, last=last:
                                       e.matmul(PS[idx % 2][:, :],
                                                Xfm[:, c * S + tb * 128: c * S + (tb + 1) * 128],
                                                vw[:, c * 1024 + dvh * 512: c * 1024 + dvh * 512 + 512],
                                                start=(c == 0), stop=last),
                                       waits=(w if c == 0 else {}),
                                       inc=("pe" if last else None))
                            if last:
                                vpe[idx] = m
                        w2 = {"pe": vpe[idx]}
                        if tb >= 2:
                            w2["sy"] = vst[tb - 2]
                        vac[idx] = P.emit("scalar", lambda e, tb=tb, dvh=dvh, idx=idx:
                                          e.copy(vsb[tb % 2][:, dvh * 512:dvh * 512 + 512],
                                                 PS[idx % 2][:, :]),
                                          waits=w2, inc="ac")
                    vst[tb] = P.emit("sync", lambda e, tb=tb: e.dma_start(
                        out=VS[tb], in_=vsb[tb % 2][:, :]),
                        waits={"ac": vac[tb * 2 + 1]}, inc="sy", inc_n=16)
                P.barrier()
                P.flush()

        if upto < 2:
            _final(P, debug)
            return nc

        with ExitStack() as s2:
            Kf = sbuf(s2, "Kf", [128, NKV * S], BF16)
            Vt = sbuf(s2, "Vt", [128, (S // 128) * NKV * HD], BF16)
            Qg = sbuf(s2, "Qg", [128, 2 * NH * TW], BF16)
            PT = sbuf(s2, "PT", [128, 16 * TW], BF16)
            Og = sbuf(s2, "Og", [128, NH * TW], BF16)
            rb_sb = [sbuf(s2, f"rbsb{i}", [128, TW], F32) for i in range(2)]

            kl = P.emit("sync", lambda e: e.dma_start(
                out=Kf[:, :].rearrange("p (k t) -> p k t", k=NKV),
                in_=KS.rearrange("k p t -> p k t")), inc="sy", inc_n=16)
            vl = P.emit("sync", lambda e: e.dma_start(
                out=Vt[:, :].rearrange("p (b v) -> p b v", b=S // 128),
                in_=VS.rearrange("b p v -> p b v")), inc="sy", inc_n=16)
            qld = {}
            qld[0] = P.emit("sync", lambda e: e.dma_start(
                out=Qg[:, :].rearrange("p (i h t) -> p i h t", i=2, h=NH)[:, 0],
                in_=QS[0].rearrange("h p t -> p h t")), inc="sy", inc_n=16)

            st_free = {0: 0, 1: 0}
            den_free = {0: 0, 1: 0}
            o_free = {0: 0, 1: 0}
            qg_lastpe = {}
            og_store = {}
            prev_pv = 0
            for g in range(TGN):
                Kg = 4 * (g + 1)
                par = g % 2
                if g + 1 < TGN:
                    w = {}
                    if g >= 1:
                        w["pe"] = qg_lastpe[g - 1]
                    qld[g + 1] = P.emit("sync", lambda e, g=g: e.dma_start(
                        out=Qg[:, :].rearrange("p (i h t) -> p i h t", i=2, h=NH)[:, (g + 1) % 2],
                        in_=QS[g + 1].rearrange("h p t -> p h t")),
                        waits=w, inc="sy", inc_n=16)
                for h in range(NH):
                    kvh = h // 4
                    hp = (g * NH + h) % 2
                    exp_mark = {}
                    stm = {}
                    den_mark = 0
                    mask_mark = 0
                    for kc in range(Kg):
                        off = max(0, kc * 128 - g * TW)
                        n = TW - off
                        w = {}
                        if kc == 0:
                            w["sy"] = max(qld[g], kl, vl)
                        if st_free[kc % 2]:
                            w["ac"] = st_free[kc % 2]
                        stm[kc] = P.emit("tensor", lambda e, kc=kc, off=off, n=n,
                                         kvh=kvh, h=h, par=par: e.matmul(
                            PS[kc % 2][:, 0:n],
                            Kf[:, kvh * S + kc * 128: kvh * S + (kc + 1) * 128],
                            Qg[:, par * NH * TW + h * TW + off: par * NH * TW + (h + 1) * TW],
                            start=True, stop=True), waits=w, inc="pe")
                        if kc >= 1:
                            kcp = kc - 1
                            offp = max(0, kcp * 128 - g * TW)
                            wden = {"ac": exp_mark[kcp]}
                            if kcp >= 4 * g:
                                wden["dv"] = mask_mark
                            if kcp == 0 and den_free[hp]:
                                wden["dv"] = max(wden.get("dv", 0), den_free[hp])
                            den_mark = P.emit("tensor", lambda e, kcp=kcp, offp=offp,
                                              hp=hp, Kg=Kg: e.matmul(
                                PS[2 + hp][:, offp:TW], ones128[:, :],
                                PT[:, kcp * TW + offp:(kcp + 1) * TW],
                                start=(kcp == 0), stop=(kcp == Kg - 1)),
                                waits=wden, inc="pe")
                        wexp = {"pe": stm[kc]}
                        if kc == 0 and prev_pv:
                            wexp["pe"] = max(stm[kc], prev_pv)
                        exp_mark[kc] = P.emit("scalar", lambda e, kc=kc, off=off, n=n:
                                              e.activation(
                            PT[:, kc * TW + off:(kc + 1) * TW], PS[kc % 2][:, 0:n],
                            AF.Exp, bias=c_shift, scale=INV_SQRT_HD),
                            waits=wexp, inc="ac")
                        st_free[kc % 2] = exp_mark[kc]
                        if kc >= 4 * g:
                            mask_mark = P.emit("vector", lambda e, kc=kc, off=off:
                                               e.tensor_mul(
                                PT[:, kc * TW + off: kc * TW + off + 128],
                                PT[:, kc * TW + off: kc * TW + off + 128],
                                umask[:, :]), waits={"ac": exp_mark[kc]}, inc="dv")
                    kcp = Kg - 1
                    offp = max(0, kcp * 128 - g * TW)
                    den_mark = P.emit("tensor", lambda e, kcp=kcp, offp=offp, hp=hp,
                                      Kg=Kg: e.matmul(
                        PS[2 + hp][:, offp:TW], ones128[:, :],
                        PT[:, kcp * TW + offp:(kcp + 1) * TW],
                        start=(kcp == 0), stop=True),
                        waits={"ac": exp_mark[kcp], "dv": mask_mark}, inc="pe")
                    pv_mark = 0
                    for kc in range(Kg):
                        off = max(0, kc * 128 - g * TW)
                        w = {}
                        if kc == 0 and o_free[hp]:
                            w["dv"] = o_free[hp]
                        pv_mark = P.emit("tensor", lambda e, kc=kc, off=off, kvh=kvh,
                                         hp=hp, Kg=Kg: e.matmul(
                            PS[6 + hp][:, off:TW],
                            Vt[:, kc * 1024 + kvh * 128: kc * 1024 + (kvh + 1) * 128],
                            PT[:, kc * TW + off:(kc + 1) * TW],
                            start=(kc == 0), stop=(kc == Kg - 1)),
                            waits=w, inc="pe")
                    prev_pv = pv_mark
                    if h == NH - 1:
                        qg_lastpe[g] = pv_mark
                    rcp_mark = P.emit("vector", lambda e, hp=hp: e.reciprocal(
                        rb_sb[hp][:, :], PS[2 + hp][:, :]),
                        waits={"pe": den_mark}, inc="dv")
                    den_free[hp] = rcp_mark
                    w = {"pe": pv_mark}
                    if (g - 1, h) in og_store:
                        w["sy"] = og_store[(g - 1, h)]
                    oev = P.emit("vector", lambda e, h=h, hp=hp: e.scalar_tensor_tensor(
                        Og[:, h * TW:(h + 1) * TW], PS[6 + hp][:, :], 1.0,
                        rb_sb[hp][:, :], ALU.mult, ALU.mult),
                        waits=w, inc="dv")
                    o_free[hp] = oev
                    og_store[(g, h)] = P.emit("sync", lambda e, g=g, h=h: e.dma_start(
                        out=OS[g, h], in_=Og[:, h * TW:(h + 1) * TW]),
                        waits={"dv": oev}, inc="sy", inc_n=16)
            P.barrier()
            P.flush()

        if upto < 3:
            _final(P, debug)
            return nc

        with ExitStack() as s3:
            Oh = sbuf(s3, "Oh", [128, NH * 1024], BF16)
            Y1 = sbuf(s3, "Y1", [128, DCN * 1024], BF16)
            wob = [sbuf(s3, f"wob{i}", [128, D], BF16) for i in range(2)]
            xrs = [sbuf(s3, f"xrs{i}", [128, 1024], F32) for i in range(2)]
            ysq = [sbuf(s3, f"ysq{i}", [128, TW], BF16) for i in range(2)]
            mrow = sbuf(s3, "mrow", [1, 1024], F32)
            erow = sbuf(s3, "erow", [1, 1024], F32)
            srow = sbuf(s3, "srow", [1, 1024], F32)
            mb_sb = sbuf(s3, "mb_sb", [128, 1024], F32)
            rs_sb = sbuf(s3, "rs_sb", [128, 1024], F32)
            ntmp = [sbuf(s3, f"ntmp{i}", [128, 1024], F32) for i in range(2)]

            x1_store = {}
            womm = {}
            y1m = {}
            prev_stat2 = 0
            prev_rows3 = 0
            prev_bcc3 = 0
            for H in range(2):
                olm = 0
                for gg in range(2):
                    w = {}
                    if H == 1:
                        w["pe"] = womm_last
                    olm = P.emit("sync", lambda e, H=H, gg=gg: e.dma_start(
                        out=Oh[:, :].rearrange("p (h i t) -> p h i t", h=NH, i=2)[:, :, gg],
                        in_=OS[2 * H + gg].rearrange("h p t -> p h t")),
                        waits=w, inc="sy", inc_n=16)
                ysq_free = {0: 0, 1: 0}
                stat_mark = {}
                for dc in range(DCN):
                    w = {}
                    if dc >= 2:
                        w["pe"] = womm[(H, dc - 2, 1)]
                    elif H == 1:
                        w["pe"] = womm[(0, DCN - 2 + dc, 1)]
                    wom = P.emit("gpsimd", lambda e, dc=dc: e.dma_start(
                        out=wob[dc % 2][:, :], in_=d["WoT"][dc]),
                        waits=w, inc="gw", inc_n=16)
                    w = {}
                    if dc >= 2:
                        w["dv"] = y1m[(H, dc - 2, 1)]
                    elif H == 1:
                        w["dv"] = y1m[(0, DCN - 2 + dc, 1)]
                    xrm = P.emit("sync", lambda e, dc=dc, H=H: e.dma_start(
                        out=xrs[dc % 2][:, :],
                        in_=d["xT"][dc * 128:(dc + 1) * 128, H * 1024:(H + 1) * 1024]),
                        waits=w, inc="sy", inc_n=16)
                    for c in range(NH):
                        for t2 in range(2):
                            w = {}
                            if c == 0 and t2 == 0:
                                w = {"gw": wom, "sy": olm}
                                if dc >= 1:
                                    w["dv"] = y1m[(H, dc - 1, 1)]
                                elif H == 1:
                                    w["dv"] = y1m[(0, DCN - 1, 1)]
                            last = (c == NH - 1)
                            m = P.emit("tensor", lambda e, c=c, t2=t2, dc=dc, last=last: e.matmul(
                                PS[t2][:, :], wob[dc % 2][:, c * 128:(c + 1) * 128],
                                Oh[:, c * 1024 + t2 * 512: c * 1024 + t2 * 512 + 512],
                                start=(c == 0), stop=last),
                                waits=w, inc=("pe" if last else None))
                            if last:
                                womm[(H, dc, t2)] = m
                    for t2 in range(2):
                        i = (dc * 2 + t2) % 2
                        ysl = slice(dc * 1024 + t2 * 512, dc * 1024 + t2 * 512 + 512)
                        y1m[(H, dc, t2)] = P.emit("vector", lambda e, t2=t2, dc=dc,
                                                  ysl=ysl: e.tensor_add(
                            Y1[:, ysl], PS[t2][:, :], xrs[dc % 2][:, t2 * 512:t2 * 512 + 512]),
                            waits={"pe": womm[(H, dc, t2)], "sy": xrm}, inc="dv")
                        w = {"dv": y1m[(H, dc, t2)]}
                        if ysq_free[i]:
                            w["pe"] = ysq_free[i]
                        elif H == 1:
                            w["pe"] = prev_stat2
                        sqm = P.emit("scalar", lambda e, i=i, ysl=ysl: e.activation(
                            ysq[i][:, :], Y1[:, ysl], AF.Square, bias=0.0, scale=1.0),
                            waits=w, inc="ac")
                        w = {"dv": y1m[(H, dc, t2)]}
                        if H == 1 and dc == 0:
                            w["dv"] = max(w["dv"], rows_dv_last)
                            w["ac"] = prev_bcc
                        P.emit("tensor", lambda e, t2=t2, dc=dc, ysl=ysl: e.matmul(
                            PS[2 + t2][0:1, :], ones_b[:, 0:1], Y1[:, ysl],
                            start=(dc == 0), stop=(dc == DCN - 1)),
                            waits=w, inc=("pe" if dc == DCN - 1 else None))
                        w2 = {"ac": sqm}
                        if H == 1 and dc == 0:
                            w2["dv"] = prev_rows3
                        m = P.emit("tensor", lambda e, t2=t2, i=i, dc=dc: e.matmul(
                            PS[4 + t2][0:1, :], ones_b[:, 0:1], ysq[i][:, :],
                            start=(dc == 0), stop=(dc == DCN - 1)),
                            waits=w2, inc="pe")
                        ysq_free[i] = m
                        if dc == DCN - 1:
                            stat_mark[t2] = m
                womm_last = womm[(H, DCN - 1, 1)]
                rowm = {}
                for t2 in range(2):
                    rsl = slice(t2 * 512, t2 * 512 + 512)
                    P.emit("vector", lambda e, t2=t2, rsl=rsl: e.tensor_scalar(
                        mrow[0:1, rsl], PS[2 + t2][0:1, :], 1.0 / D, None, ALU.mult),
                        waits={"pe": stat_mark[1]}, inc="dv")
                    rowm[t2] = P.emit("vector", lambda e, t2=t2, rsl=rsl: e.tensor_scalar(
                        erow[0:1, rsl], PS[4 + t2][0:1, :], 1.0 / D, None, ALU.mult),
                        inc="dv")
                sqm = P.emit("scalar", lambda e: e.activation(
                    srow[0:1, :], mrow[0:1, :], AF.Square, bias=0.0, scale=1.0),
                    waits={"dv": rowm[1]}, inc="ac")
                vv = P.emit("vector", lambda e: e.tensor_sub(
                    erow[0:1, :], erow[0:1, :], srow[0:1, :]),
                    waits={"ac": sqm}, inc="dv")
                sdm = P.emit("scalar", lambda e: e.activation(
                    srow[0:1, :], erow[0:1, :], AF.Sqrt, bias=c_eps, scale=1.0),
                    waits={"dv": vv}, inc="ac")
                rows_dv_last = P.emit("vector", lambda e: e.reciprocal(
                    srow[0:1, :], srow[0:1, :]), waits={"ac": sdm}, inc="dv")
                bcm = {}
                for t2 in range(2):
                    rsl = slice(t2 * 512, t2 * 512 + 512)
                    P.emit("tensor", lambda e, t2=t2, rsl=rsl: e.matmul(
                        PS[6 + t2][:, :], onesf[:, :], mrow[0:1, rsl],
                        start=True, stop=True),
                        waits={"dv": rows_dv_last}, inc="pe")
                    bcm[t2] = P.emit("tensor", lambda e, t2=t2, rsl=rsl: e.matmul(
                        PS[2 + t2][:, :], onesf[:, :], srow[0:1, rsl],
                        start=True, stop=True), inc="pe")
                bcc = {}
                for t2 in range(2):
                    P.emit("scalar", lambda e, t2=t2: e.copy(
                        mb_sb[:, t2 * 512:t2 * 512 + 512], PS[6 + t2][:, :]),
                        waits={"pe": bcm[1]}, inc="ac")
                    bcc[t2] = P.emit("scalar", lambda e, t2=t2: e.copy(
                        rs_sb[:, t2 * 512:t2 * 512 + 512], PS[2 + t2][:, :]),
                        inc="ac")
                for dc in range(DCN):
                    i = dc % 2
                    ysl = slice(dc * 1024, (dc + 1) * 1024)
                    w = {"ac": bcc[1]}
                    P.emit("vector", lambda e, i=i, ysl=ysl: e.tensor_sub(
                        ntmp[i][:, :], Y1[:, ysl], mb_sb[:, :]), waits=w, inc="dv")
                    x1m = P.emit("vector", lambda e, i=i, ysl=ysl: e.tensor_mul(
                        Y1[:, ysl], ntmp[i][:, :], rs_sb[:, :]), inc="dv")
                    x1_store[(H, dc)] = P.emit("sync", lambda e, H=H, dc=dc, ysl=ysl:
                                               e.dma_start(
                        out=X1S[2 * H:2 * H + 2, dc].rearrange("g p t -> p g t"),
                        in_=Y1[:, ysl].rearrange("p (g t) -> p g t", g=2)),
                        waits={"dv": x1m}, inc="sy", inc_n=16)
                prev_bcc = bcc[1]
                prev_stat2 = stat_mark[1]
                prev_rows3 = rows_dv_last
                prev_bcc3 = bcc[1]
            P.barrier()
            P.flush()

        if upto < 4:
            _final(P, debug)
            return nc

        with ExitStack() as s4:
            X1g = sbuf(s4, "X1g", [128, DCN * TW], BF16)
            Hb = sbuf(s4, "Hb", [128, ICN * TW], BF16)
            wbig = sbuf(s4, "wbig", [128, 2 * IT], BF16)
            wg = [wbig[:, i * D:(i + 1) * D] for i in range(2)]
            wu = [wbig[:, (2 + i) * D:(3 + i) * D] for i in range(2)]
            wd = [wbig[:, i * IT:(i + 1) * IT] for i in range(2)]
            silu_sb = [sbuf(s4, f"silu{i}", [128, TW], BF16) for i in range(2)]
            ysq4 = [sbuf(s4, f"ysq4_{i}", [128, TW], BF16) for i in range(2)]
            mrow4 = sbuf(s4, "mrow4", [1, TW], F32)
            erow4 = sbuf(s4, "erow4", [1, TW], F32)
            srow4 = sbuf(s4, "srow4", [1, TW], F32)
            mb4 = sbuf(s4, "mb4", [128, TW], F32)
            rs4 = sbuf(s4, "rs4", [128, TW], F32)
            ntmp4 = [sbuf(s4, f"ntmp4_{i}", [128, TW], F32) for i in range(2)]
            ostg = [sbuf(s4, f"ostg{i}", [128, TW], F32) for i in range(2)]

            prev_norm_dv = 0
            prev_dn_pe = 0
            prev_rows_dv = 0
            prev_silu_last = 0
            prev_h_last = 0
            prev_bcc4 = 0
            prev_st4 = 0
            prev_out_st = {}
            for tg in range(TGN):
                w = {}
                if prev_norm_dv:
                    w["dv"] = prev_norm_dv
                x1l = P.emit("sync", lambda e, tg=tg: e.dma_start(
                    out=X1g[:, :].rearrange("p (c t) -> p c t", c=DCN),
                    in_=X1S[tg].rearrange("c p t -> p c t")),
                    waits=w, inc="sy", inc_n=16)
                if tg == 0:
                    x1l = max(x1l, 0)
                gmm = {}
                umm = {}
                silu_m = {0: 0, 1: 0}
                h_m = {}
                for ic in range(ICN):
                    b = ic % 2
                    w = {}
                    if ic >= 2:
                        w["pe"] = umm[ic - 2]
                    elif prev_dn_pe:
                        w["pe"] = prev_dn_pe
                    wgm = P.emit("gpsimd", lambda e, ic=ic, b=b: e.dma_start(
                        out=wg[b][:, :], in_=d["WgT"][ic]), waits=w, inc="gw", inc_n=16)
                    wgm = P.emit("gpsimd", lambda e, ic=ic, b=b: e.dma_start(
                        out=wu[b][:, :], in_=d["WuT"][ic]), inc="gw", inc_n=16)
                    for c in range(DCN):
                        w = {}
                        if c == 0:
                            w = {"gw": wgm, "sy": x1l}
                            if ic >= 2:
                                w["ac"] = silu_m[b]
                            elif prev_silu_last:
                                w["ac"] = prev_silu_last
                        last = (c == DCN - 1)
                        m = P.emit("tensor", lambda e, c=c, b=b, last=last: e.matmul(
                            PS[0 + b][:, :], wg[b][:, c * 128:(c + 1) * 128],
                            X1g[:, c * TW:(c + 1) * TW], start=(c == 0), stop=last),
                            waits=w, inc=("pe" if last else None))
                        if last:
                            gmm[ic] = m
                    for c in range(DCN):
                        w = {}
                        if c == 0:
                            if ic >= 2:
                                w["dv"] = h_m[ic - 2]
                            elif prev_h_last:
                                w["dv"] = prev_h_last
                        last = (c == DCN - 1)
                        m = P.emit("tensor", lambda e, c=c, b=b, last=last: e.matmul(
                            PS[2 + b][:, :], wu[b][:, c * 128:(c + 1) * 128],
                            X1g[:, c * TW:(c + 1) * TW], start=(c == 0), stop=last),
                            waits=w, inc=("pe" if last else None))
                        if last:
                            umm[ic] = m
                    w = {"pe": gmm[ic]}
                    if ic >= 2:
                        w["dv"] = h_m[ic - 2]
                    silu_m[b] = P.emit("scalar", lambda e, ic=ic, b=b: e.activation(
                        silu_sb[b][:, :], PS[0 + b][:, :], AF.Silu,
                        bias=0.0, scale=sg[:, ic:ic + 1]), waits=w, inc="ac")
                    hw = {"pe": umm[ic], "ac": silu_m[b]}
                    if prev_dn_pe and ic < 2:
                        hw["pe"] = max(hw["pe"], prev_dn_pe)
                    h_m[ic] = P.emit("vector", lambda e, ic=ic, b=b:
                                     e.scalar_tensor_tensor(
                        Hb[:, ic * TW:(ic + 1) * TW], PS[2 + b][:, :],
                        su[:, ic:ic + 1], silu_sb[b][:, :], ALU.mult, ALU.mult),
                        waits=hw, inc="dv")
                dnm = {}
                y2m = {}
                ysq_free4 = {0: 0, 1: 0}
                st4 = 0
                for dc in range(DCN):
                    b = dc % 2
                    w = {"pe": umm[ICN - 1]}
                    if dc >= 2:
                        w["pe"] = dnm[dc - 2]
                    wdm = P.emit("gpsimd", lambda e, dc=dc, b=b: e.dma_start(
                        out=wd[b][:, :], in_=d["WdT"][dc]), waits=w, inc="gw", inc_n=16)
                    for c in range(ICN):
                        w = {}
                        if c == 0:
                            w = {"gw": wdm, "dv": h_m[ICN - 1]}
                            if dc >= 2:
                                w["dv"] = max(w["dv"], y2m[dc - 2])
                            elif prev_bcc4:
                                w["ac"] = prev_bcc4
                        last = (c == ICN - 1)
                        m = P.emit("tensor", lambda e, c=c, b=b, last=last: e.matmul(
                            PS[4 + b][:, :], wd[b][:, c * 128:(c + 1) * 128],
                            Hb[:, c * TW:(c + 1) * TW], start=(c == 0), stop=last),
                            waits=w, inc=("pe" if last else None))
                        if last:
                            dnm[dc] = m
                    y2m[dc] = P.emit("vector", lambda e, dc=dc, b=b:
                                     e.scalar_tensor_tensor(
                        X1g[:, dc * TW:(dc + 1) * TW], PS[4 + b][:, :],
                        sd[:, dc:dc + 1], X1g[:, dc * TW:(dc + 1) * TW],
                        ALU.mult, ALU.add), waits={"pe": dnm[dc]}, inc="dv")
                    w = {"dv": y2m[dc]}
                    if ysq_free4[b]:
                        w["pe"] = ysq_free4[b]
                    elif prev_st4:
                        w["pe"] = prev_st4
                    sqm = P.emit("scalar", lambda e, dc=dc, b=b: e.activation(
                        ysq4[b][:, :], X1g[:, dc * TW:(dc + 1) * TW], AF.Square,
                        bias=0.0, scale=1.0), waits=w, inc="ac")
                    w = {"dv": y2m[dc]}
                    if dc == 0 and prev_rows_dv:
                        w["dv"] = max(w["dv"], prev_rows_dv)
                    P.emit("tensor", lambda e, dc=dc: e.matmul(
                        PS[6][0:1, :], ones_b[:, 0:1], X1g[:, dc * TW:(dc + 1) * TW],
                        start=(dc == 0), stop=(dc == DCN - 1)),
                        waits=w, inc=("pe" if dc == DCN - 1 else None))
                    w2 = {"ac": sqm}
                    if dc == 0 and prev_rows_dv:
                        w2["dv"] = prev_rows_dv
                    m = P.emit("tensor", lambda e, dc=dc, b=b: e.matmul(
                        PS[7][0:1, :], ones_b[:, 0:1], ysq4[b][:, :],
                        start=(dc == 0), stop=(dc == DCN - 1)),
                        waits=w2, inc="pe")
                    ysq_free4[b] = m
                    if dc == DCN - 1:
                        st4 = m
                prev_dn_pe = dnm[DCN - 1]
                P.emit("vector", lambda e: e.tensor_scalar(
                    mrow4[0:1, :], PS[6][0:1, :], 1.0 / D, None, ALU.mult),
                    waits={"pe": st4}, inc="dv")
                rm = P.emit("vector", lambda e: e.tensor_scalar(
                    erow4[0:1, :], PS[7][0:1, :], 1.0 / D, None, ALU.mult), inc="dv")
                sqm = P.emit("scalar", lambda e: e.activation(
                    srow4[0:1, :], mrow4[0:1, :], AF.Square, bias=0.0, scale=1.0),
                    waits={"dv": rm}, inc="ac")
                vv = P.emit("vector", lambda e: e.tensor_sub(
                    erow4[0:1, :], erow4[0:1, :], srow4[0:1, :]),
                    waits={"ac": sqm}, inc="dv")
                sdm = P.emit("scalar", lambda e: e.activation(
                    srow4[0:1, :], erow4[0:1, :], AF.Sqrt, bias=c_eps, scale=1.0),
                    waits={"dv": vv}, inc="ac")
                rows_dv = P.emit("vector", lambda e: e.reciprocal(
                    srow4[0:1, :], srow4[0:1, :]), waits={"ac": sdm}, inc="dv")
                prev_rows_dv = rows_dv
                w = {"dv": max(rows_dv, y2m[DCN - 1])}
                P.emit("tensor", lambda e: e.matmul(
                    PS[4][:, :], onesf[:, :], mrow4[0:1, :], start=True, stop=True),
                    waits=w, inc="pe")
                bcm4 = P.emit("tensor", lambda e: e.matmul(
                    PS[5][:, :], onesf[:, :], srow4[0:1, :], start=True, stop=True),
                    inc="pe")
                P.emit("scalar", lambda e: e.copy(mb4[:, :], PS[4][:, :]),
                       waits={"pe": bcm4}, inc="ac")
                bcc4 = P.emit("scalar", lambda e: e.copy(rs4[:, :], PS[5][:, :]),
                              inc="ac")
                norm_last = 0
                for dc in range(DCN):
                    i = dc % 2
                    w = {"ac": bcc4}
                    if (tg - 1, dc) in prev_out_st:
                        pass
                    if dc >= 2:
                        w["sy"] = prev_out_st[(tg, dc - 2)]
                    elif (tg - 1, DCN - 2 + dc) in prev_out_st:
                        w["sy"] = prev_out_st[(tg - 1, DCN - 2 + dc)]
                    P.emit("vector", lambda e, i=i, dc=dc: e.tensor_sub(
                        ntmp4[i][:, :], X1g[:, dc * TW:(dc + 1) * TW], mb4[:, :]),
                        waits=w, inc="dv")
                    norm_last = P.emit("vector", lambda e, i=i: e.tensor_mul(
                        ostg[i][:, :], ntmp4[i][:, :], rs4[:, :]), inc="dv")
                    prev_out_st[(tg, dc)] = P.emit("sync",
                                                   lambda e, i=i, dc=dc, tg=tg:
                                                   e.dma_start(
                        out=outT[dc * 128:(dc + 1) * 128, tg * TW:(tg + 1) * TW],
                        in_=ostg[i][:, :]), waits={"dv": norm_last},
                        inc="sy", inc_n=16)
                prev_norm_dv = norm_last
                prev_silu_last = silu_m[1] if ICN % 2 == 0 else max(silu_m[0], silu_m[1])
                prev_h_last = h_m[ICN - 1]
                prev_bcc4 = bcc4
                prev_st4 = st4
            P.barrier()
            P.flush()

        _final(P, debug)
    return nc


def _final(P, debug):
    P.emit("sync", None, waits={"sy": P.cnt["sy"]})
    P.emit("gpsimd", None, waits={"gw": P.cnt["gw"]})
    P.flush()


def kernel(**inputs):
    shared, xTs = host_prep(**inputs)
    nc = build_nc(upto=4, debug=False)
    in_maps = [dict(shared, xT=xTs[b]) for b in range(8)]
    res = run_bass_kernel_spmd(nc, in_maps, core_ids=list(range(8)))
    out = np.stack([np.asarray(r["outT"]).T for r in res.results], axis=0)
    return np.ascontiguousarray(out.astype(np.float32))
